# version 22
# baseline (speedup 1.0000x reference)
import numpy as np
import ml_dtypes
from contextlib import ExitStack

import concourse.bass as bass
import concourse.mybir as mybir
from concourse.bass_utils import run_bass_kernel_spmd
from concourse.bass import IndirectOffsetOnAxis

F32 = mybir.dt.float32
BF16 = mybir.dt.bfloat16
U32 = mybir.dt.uint32
AF = mybir.ActivationFunctionType
ALU = mybir.AluOpType
AX = mybir.AxisListType
BF = ml_dtypes.bfloat16

L = 4096
NCORES = 8


def _host_tables():
    t = {}
    n1 = np.arange(64)
    G = np.zeros((64, 64), np.float64)
    for q in range(33):
        G[:, q] = np.cos(2 * np.pi * n1 * q / 64.0)
    for k1 in range(1, 32):
        G[:, 32 + k1] = -np.sin(2 * np.pi * n1 * k1 / 64.0)
    GA = np.zeros((128, 128), np.float64)
    GA[:64, :64] = G
    GA[64:, 64:] = G
    t["ga"] = GA.astype(np.float32).astype(BF)

    n2 = np.arange(64)
    SB = np.zeros((33, 128, 128), np.float64)
    for q in range(33):
        k = q + 64 * np.arange(64)
        th = 2 * np.pi * np.outer(n2, k) / L
        SB[q, :64, :64] = np.cos(th)
        SB[q, 64:, :64] = np.sin(th)
        SB[q, :64, 64:] = -np.sin(th)
        SB[q, 64:, 64:] = np.cos(th)
    t["sbt"] = SB.reshape(33 * 128, 128).astype(np.float32).astype(BF)

    k2 = np.arange(64)
    tlo = np.arange(64)
    ang = 2 * np.pi * np.outer(k2, tlo) / 64.0
    SIB = np.zeros((64, 128), np.float64)
    SIB[:, :64] = np.cos(ang)
    SIB[:, 64:] = np.sin(ang)
    t["sib"] = SIB.astype(np.float32).astype(BF)

    m = np.full(33, 2.0)
    m[0] = 1.0
    m[32] = 1.0
    k1v = np.arange(33)
    thi = np.arange(64)
    MIA = np.zeros((66, 64 * 64), np.float64)
    for tl in range(64):
        phi = 2 * np.pi * np.outer(k1v, tl + 64 * thi) / L
        MIA[:33, tl * 64:(tl + 1) * 64] = (m[:, None] / L) * np.cos(phi)
        MIA[33:, tl * 64:(tl + 1) * 64] = -(m[:, None] / L) * np.sin(phi)
    t["mia"] = MIA.astype(np.float32).astype(BF)

    t["identf"] = np.eye(128, dtype=np.float32)
    t["iota4"] = np.tile(np.arange(4, dtype=np.float32), (128, 1))
    t["iota4"] = np.ascontiguousarray(t["iota4"])
    return t


class Cfg:
    def __init__(self, NP=4, CH=4):
        self.NP = NP
        self.CH = CH
        assert CH % 2 == 0 and CH <= 4
        self.PAIRS = CH // 2
        self.R = NP * CH * 128


def build_nc(cfg):
    NP, CH, PAIRS, R = cfg.NP, cfg.CH, cfg.PAIRS, cfg.R
    PR = CH * 128
    H2 = 2 * CH

    nc = bass.Bass("TRN2", target_bir_lowering=False)

    x_d = nc.dram_tensor("x", [R, L], F32, kind="ExternalInput").ap()
    wt_d = nc.dram_tensor("wt", [L, L], F32, kind="ExternalInput").ap()
    ag_d = nc.dram_tensor("ag", [L, 36], F32, kind="ExternalInput").ap()
    bcat_d = nc.dram_tensor("bcat", [32, L], F32, kind="ExternalInput").ap()
    ebias_d = nc.dram_tensor("ebias", [4, 1], F32, kind="ExternalInput").ap()
    ga_d = nc.dram_tensor("ga", [128, 128], BF16, kind="ExternalInput").ap()
    sbt_d = nc.dram_tensor("sbt", [33 * 128, 128], BF16,
                           kind="ExternalInput").ap()
    sib_d = nc.dram_tensor("sib", [64, 128], BF16, kind="ExternalInput").ap()
    mia_d = nc.dram_tensor("mia", [66, 4096], BF16, kind="ExternalInput").ap()
    identf_d = nc.dram_tensor("identf", [128, 128], F32,
                              kind="ExternalInput").ap()
    iota4_d = nc.dram_tensor("iota4", [128, 4], F32, kind="ExternalInput").ap()
    out_d = nc.dram_tensor("out", [R, L], F32, kind="ExternalOutput").ap()

    xn2_d = [nc.dram_tensor("xn2_%d" % p, [CH * 128, 8192], BF16,
                            kind="Internal").ap() for p in range(NP)]
    xn2t_d = [nc.dram_tensor("xn2t_%d" % p, [L, CH * 128], BF16,
                             kind="Internal").ap() for p in range(NP)]
    ascr_d = nc.dram_tensor("ascr", [128, PAIRS * 8192], BF16,
                            kind="Internal").ap()
    escr_d = nc.dram_tensor("escr", [66 * CH, 64 * 128], BF16,
                            kind="Internal").ap()
    wscr_d = nc.dram_tensor("wscr", [L, L], BF16, kind="Internal").ap()

    N_COMB = PAIRS * 4
    N_BMOV = 64
    N_XN2 = 2 * H2
    N_ASCR = PAIRS * 4
    N_OUT = 4 * CH
    NW = 128

    bmov_pcum = []
    pc = [0, 0]
    for k1 in range(33):
        pc[k1 % 2] += 2 + (0 if k1 in (0, 32) else 2)
        bmov_pcum.append(pc[k1 % 2])
    BMOV_PP = (pc[0], pc[1])
    SBT_PP = (17, 16)
    ESCR_PP = (34, 32)

    def apply_on_v(cc):
        return True

    def applv_cum(p, cc):
        return (p * CH + cc + 1) * 8

    def appl_done_wait(eng, gch):
        if gch < 0:
            return
        eng.wait_ge(S["applv"], (gch + 1) * 8)

    def roll_free_wait(eng, gk):
        prev = gk - 2
        if prev < 0:
            return
        eng.wait_ge(S["applv"], prev + 1)

    sem_names = [
        "x0", "x1", "comb0", "comb1", "sbt0", "sbt1", "bmov0", "bmov1",
        "et0", "et1", "xnfr", "w0", "w1", "w2", "w3", "wg0", "wg1",
        "xn2_0", "xn2_1",
        "ascr0", "ascr1",
        "escr0", "escr1", "out0", "out1", "wscr0", "wscr1",
        "gath0", "gath1",
        "vpre", "sqrt", "vstat", "act", "xres", "mz",
        "amm", "aev", "bmm", "xev", "sq", "pev", "ibmm", "eev",
        "iamm", "cev", "top", "exp", "sm", "applv", "applg", "fev",
        "zmm", "lgc", "zev", "t1", "lgt", "t2", "pzt", "zgt",
        "t3", "zgs", "fmm", "wcons", "iot", "const",
        "chv", "chs", "chg", "cg", "xal0", "xal1",
        "xbt0", "xbt1", "xtw0", "xtw1", "mul",
    ]
    es = ExitStack()
    S = {n: es.enter_context(nc.semaphore(name="s_" + n)) for n in sem_names}

    def sb(name, shape, dtype, **kw):
        return es.enter_context(nc.sbuf_tensor(name, shape, dtype, **kw))

    def psum(name, shape, dtype):
        return es.enter_context(nc.psum_tensor(name, shape, dtype))

    xf = [sb("xf%d" % i, [128, 2048], F32) for i in range(2)]
    xnr = [sb("xnr%d" % i, [128, 2048], BF16) for i in range(2)]
    xa = sb("xa", [128, CH * 4096], BF16)
    xat = [sb("xat%d" % i, [128, 2048], BF16) for i in range(2)]
    abuf = sb("abuf", [128, 2048], BF16)
    mt = [sb("mt%d" % i, [128, PR], BF16) for i in range(2)]
    xh = [sb("xh%d" % i, [128, PR], BF16) for i in range(2)]
    t1b = [sb("t1b", [64, PR], BF16)] * 2
    t2b = [sb("t2b%d" % i, [64, PR], BF16) for i in range(2)]
    arena = sb("arena", [128, 32 * PR], BF16, is_transpose_dest=True)
    etile = [sb("etile%d" % i, [128, PR], BF16) for i in range(2)]
    et = [sb("et%d" % i, [66, 8192], BF16) for i in range(2)]
    tmpt = [sb("tmpt%d" % i, [128, PR], BF16, is_transpose_dest=True)
            for i in range(2)]
    corr = sb("corr", [128, 4096], BF16)
    roll = [sb("roll%d" % i, [128, 4096], BF16) for i in range(2)]
    wtile = [sb("wtile%d" % i, [128, 1024], BF16) for i in range(2)]
    outh = [sb("outh%d" % i, [128, 1024], F32) for i in range(2)]
    agb = sb("agb", [128, 32 * 36], BF16)
    bcatb = sb("bcatb", [32, 4096], BF16)
    ebias = sb("ebias_s", [4, 1], F32)
    gab = sb("gab", [128, 128], BF16)
    sbtb = [sb("sbtb%d" % i, [128, 128], BF16) for i in range(2)]
    sibb = sb("sibb", [128, 128], BF16)
    miab = sb("miab", [66, 4096], BF16)
    identf = sb("identf_s", [128, 128], F32)
    iota4 = sb("iota4_s", [128, 4], F32)
    rowb = sb("rowb", [128, CH], U32)
    stat6 = sb("stat6", [128, 48], F32)
    mv2 = sb("mv2", [128, 2], F32)
    sv = sb("sv", [128, 1], F32)
    rstd = sb("rstd", [128, 1], F32)
    nmb = sb("nmb", [128, 1], F32)
    v8 = [sb("v8_%d" % i, [128, 8], F32) for i in range(2)]
    idx8 = [sb("idx8_%d" % i, [128, 8], U32) for i in range(2)]
    tau = [sb("tau%d" % i, [128, 8], U32) for i in range(2)]
    tt1 = sb("tt1", [128, 8], U32)
    tt2 = sb("tt2", [128, 8], U32)
    offs = [sb("offs%d" % i, [128, 8], U32) for i in range(2)]
    negv0 = [sb("negv0_%d" % i, [128, 1], F32) for i in range(2)]
    e8 = [sb("e8_%d" % i, [128, 8], F32) for i in range(2)]
    s8 = sb("s8", [128, 1], F32)
    r8 = sb("r8", [128, 1], F32)
    w8 = [sb("w8_%d" % i, [128, 8], F32) for i in range(2)]
    zt = sb("zt", [32, PR], F32)
    lg = sb("lg", [4, PR], F32)
    lgt = sb("lgt", [128, 4 * CH], F32)
    pad8 = sb("pad8", [128, 8], F32)
    gv8 = sb("gv8", [128, 8], F32)
    pidx = sb("pidx", [128, 8], U32)
    idxf = sb("idxf", [128, 1], F32)
    oh = sb("oh", [128, 4], F32)
    pzt = sb("pzt", [128, 32 * CH], F32)
    zgt = sb("zgt", [128, 32 * CH], F32)
    zgs = sb("zgs", [32, PR], BF16)

    pb = [psum("pb%d" % i, [128, 512], F32) for i in range(8)]

    def p_slot(k1):
        base = 64 * (k1 % 2)
        col = (k1 // 2) * PR
        return arena[base:base + 64, col:col + PR]

    NCONST_DMA = 7 * 16


    class Chain:
        def __init__(self, eng, sem):
            self.e, self.sem, self.n = eng, sem, 0
            self.last = None

        def pre(self):
            if self.last is not None:
                self.e.wait_ge(*self.last)

        def post(self, ins, sem=None, count=None, inc=1):
            if sem is None:
                self.n += 1
                ins.then_inc(self.sem, 1)
                self.last = (self.sem, self.n)
            else:
                ins.then_inc(sem, inc)
                self.last = (sem, count)
            return ins

    with nc.Block() as block:

        @block.sync
        def _(eng):
            eng.dma_start(gab[:], ga_d).then_inc(S["const"], 16)
            eng.dma_start(sibb[0:64, :], sib_d).then_inc(S["const"], 16)
            eng.dma_start(sibb[64:128, :], sib_d).then_inc(S["const"], 16)
            eng.dma_start(miab[:], mia_d).then_inc(S["const"], 16)
            eng.dma_start(identf[:], identf_d).then_inc(S["const"], 16)
            eng.dma_start(iota4[:], iota4_d).then_inc(S["const"], 16)
            eng.dma_start(ebias[:], ebias_d).then_inc(S["const"], 16)


            def x_load_s(p, cc):
                pbase2 = p * PR
                for half in range(2):
                    h = cc * 2 + half
                    gi = p * H2 + h
                    if gi >= 2:
                        pi = gi - 2
                        eng.wait_ge(S["vstat"], pi // 2 + 1)
                        eng.wait_ge(S["act"], pi + 1)
                    rows = x_d[pbase2 + cc * 128: pbase2 + (cc + 1) * 128,
                               half * 2048:(half + 1) * 2048]
                    eng.dma_start(xf[h % 2][:], rows
                                  ).then_inc(S["x%d" % (h % 2)], 16)

            def xbar_chain(pp):
                    eng.wait_ge(S["xn2_0"], 16 * 2 * CH * (pp + 1))
                    eng.wait_ge(S["xn2_1"], 16 * 2 * CH * (pp + 1))
                    for fc in range(32):
                        gi = pp * 32 + fc
                        if gi >= 2:
                            eng.wait_ge(S["xtw%d" % (fc % 2)],
                                            16 * (pp * 16 + fc // 2))
                        eng.dma_start(tmpt[fc % 2][:],
                                          xn2_d[pp][0:PR, fc * 128:(fc + 1) * 128],
                                          transpose=True
                                          ).then_inc(S["xbt%d" % (fc % 2)], 16)
                        eng.wait_ge(S["xbt%d" % (fc % 2)],
                                        16 * (pp * 16 + fc // 2 + 1))
                        eng.dma_start(xn2t_d[pp][fc * 128:(fc + 1) * 128, :],
                                          tmpt[fc % 2][:]
                                          ).then_inc(S["xtw%d" % (fc % 2)], 16)

            x_load_s(0, 0)
            X0_DONE = True
            for p in range(NP):
                pbase = p * PR
                if p == 0:
                    for cc in range(1, CH):
                        x_load_s(0, cc)
                if p + 1 < NP:
                    x_load_s(p + 1, 0)
                if p == 0:
                    xbar_chain(0)

                eng.wait_ge(S["xtw0"], 16 * 16 * (p + 1))
                eng.wait_ge(S["xtw1"], 16 * 16 * (p + 1))
                tv = xn2t_d[p].rearrange("(n v) r -> n v r", n=64, v=64)
                for cp in range(PAIRS):
                    for q in range(4):
                        lq = cp * 4 + q
                        if (p + 1 < NP and lq % 2 == 0 and 0 < lq // 2 < CH):
                            x_load_s(p + 1, lq // 2)
                        qi = (p * PAIRS + cp) * 4 + q
                        if qi >= 2:
                            eng.wait_ge(S["amm"], 4 * (qi - 1))
                        for g in range(2):
                            rs = cp * 256 + g * 128
                            src = tv[:, 16 * q:16 * (q + 1), rs:rs + 128]
                            dst = xat[q % 2][64 * g:64 * (g + 1), :].rearrange(
                                "n (v r) -> n v r", v=16, r=128)
                            eng.dma_start(dst, src).then_inc(
                                S["comb%d" % (q % 2)], 16)

                av = ascr_d.rearrange("(g kp) (cp v r) -> kp v cp g r",
                                      g=2, cp=PAIRS, v=64, r=128)
                for k1 in range(33):
                    gi = p * 33 + k1
                    if gi >= 2:
                        eng.wait_ge(S["bmm"], gi - 1)
                    eng.dma_start(sbtb[k1 % 2][:],
                                  sbt_d[k1 * 128:(k1 + 1) * 128, :]
                                  ).then_inc(S["sbt%d" % (k1 % 2)], 16)
                    eng.wait_ge(S["ascr0"], 16 * (p + 1) * (N_ASCR // 2))
                    eng.wait_ge(S["ascr1"], 16 * (p + 1) * (N_ASCR // 2))
                    mtv0 = mt[k1 % 2][0:64, :].rearrange(
                        "v (cp g r) -> v cp g r", cp=PAIRS, g=2, r=128)
                    for g in range(2):
                        eng.dma_start(mtv0[:, :, g, :], av[k1][:, :, g, :]
                                      ).then_inc(S["bmov%d" % (k1 % 2)], 16)
                    if k1 not in (0, 32):
                        mtv1 = mt[k1 % 2][64:128, :].rearrange(
                            "v (cp g r) -> v cp g r", cp=PAIRS, g=2, r=128)
                        for g in range(2):
                            eng.dma_start(mtv1[:, :, g, :],
                                          av[32 + k1][:, :, g, :]
                                          ).then_inc(S["bmov%d" % (k1 % 2)],
                                                     16)

                eng.wait_ge(S["ibmm"], 33 * (p + 1))
                if p >= 1:
                    eng.wait_ge(S["fmm"], p * 4 * CH)
                for fc in range(32):
                    src = xn2t_d[p][fc * 128:(fc + 1) * 128, :]
                    eng.dma_start(arena[:, fc * PR:(fc + 1) * PR], src
                                  ).then_inc(S["xnfr"], 16)

                if p + 1 < NP:
                    xbar_chain(p + 1)
                for oq in range(4):
                    for fc in range(32):
                        wi = p * NW + oq * 32 + fc
                        idx = oq * 32 + fc
                        if idx == 0:
                            eng.wait_ge(S["wscr0"], 16 * (NW // 2))
                            eng.wait_ge(S["wscr1"], 16 * (NW // 2))
                        if wi % 2 == 1:
                            continue
                        if wi >= 2:
                            eng.wait_ge(S["wcons"], wi - 1)
                        eng.dma_start(
                            wtile[0][:],
                            wscr_d[fc * 128:(fc + 1) * 128,
                                   oq * 1024:(oq + 1) * 1024]
                        ).then_inc(S["w0"], 16)

            eng.wait_ge(S["out0"], 16 * NP * (N_OUT // 2))
            eng.wait_ge(S["out1"], 16 * NP * (N_OUT // 2))

        @block.scalar
        def _(eng):
            ch_ = Chain(eng, S["chs"])
            eng.wait_ge(S["const"], NCONST_DMA)

            def ln_s(p, cc):
                eng.wait_ge(S["vpre"], p * CH + cc + 1)
                ch_.pre()
                ch_.post(eng.sqrt(sv[:], sv[:]), S["sqrt"], p * CH + cc + 1)
                for half in range(2):
                    h = cc * 2 + half
                    gi = p * H2 + h
                    eng.wait_ge(S["act"], gi + 1)
                    rows = slice(cc * 128, (cc + 1) * 128)
                    eng.dma_start(
                        xn2_d[p][rows, half * 2048:(half + 1) * 2048],
                        xnr[h % 2][:]).then_inc(S["xn2_%d" % (h % 2)], 16)
                    eng.dma_start(
                        xn2_d[p][rows, 4096 + half * 2048:
                              4096 + (half + 1) * 2048],
                        xnr[h % 2][:]).then_inc(S["xn2_%d" % (h % 2)], 16)

            for cc in range(CH):
                ln_s(0, cc)
            for p in range(NP):
                pbase = p * PR

                if p >= 1:
                    eng.wait_ge(S["bmov0"], 16 * p * BMOV_PP[0])
                    eng.wait_ge(S["bmov1"], 16 * p * BMOV_PP[1])
                for cp in range(PAIRS):
                    for t in range(16):
                        lt = cp * 16 + t
                        if p + 1 < NP and lt % 8 == 0 and lt // 8 < CH:
                            ln_s(p + 1, lt // 8)
                        gi = (p * PAIRS + cp) * 16 + t
                        eng.wait_ge(S["amm"], gi + 1)
                        qq = t // 4
                        grp = (p * PAIRS + cp) * 4 + qq
                        if grp >= 1 and t % 4 == 0:
                            eng.wait_ge(S["ascr%d" % ((grp - 1) % 2)],
                                        16 * ((grp - 1) // 2 + 1))
                        dst = abuf[:, (t % 4) * 512:(t % 4 + 1) * 512]
                        ch_.pre()
                        ch_.post(eng.copy(dst, pb[(0, 1, 4, 5)[t % 4]][:]),
                                 S["aev"], gi + 1)
                        if t % 4 == 3:
                            ov = ascr_d[:, cp * 8192 + qq * 2048:
                                        cp * 8192 + (qq + 1) * 2048]
                            ch_.pre()
                            eng.dma_start(ov, abuf[:]
                                          ).then_inc(
                                              S["ascr%d" % (grp % 2)], 16)

                for k1 in range(33):
                    gi = p * 33 + k1
                    eng.wait_ge(S["bmm"], gi + 1)
                    if gi >= 2:
                        eng.wait_ge(S["pev"], gi - 1)
                    ch_.pre()
                    ch_.post(eng.copy(xh[k1 % 2][:],
                                      pb[(2, 3, 6, 7)[k1 % 4]][:, 0:PR]),
                             S["xev"], gi + 1)
                    ch_.pre()
                    ch_.post(eng.square(t2b[k1 % 2][:],
                                        xh[k1 % 2][64:128, :]),
                             S["sq"], gi + 1)

                if p >= 1:
                    eng.wait_ge(S["et0"], 16 * (p * CH // 2))
                    eng.wait_ge(S["et1"], 16 * (p * CH // 2))
                for k1 in range(33):
                    gi = p * 33 + k1
                    eng.wait_ge(S["ibmm"], gi + 1)
                    if gi >= 2:
                        eng.wait_ge(S["escr%d" % (k1 % 2)],
                                    16 * (p * ESCR_PP[k1 % 2] +
                                          2 * (k1 // 2)))
                    ch_.pre()
                    ch_.post(eng.copy(etile[k1 % 2][:],
                                      pb[4 + k1 % 2][:, 0:PR]),
                             S["eev"], gi + 1)
                    ch_.pre()
                    for hh in range(2):
                        kh = hh * 33 + k1
                        eng.dma_start(
                            escr_d[kh * CH:(kh + 1) * CH, :].rearrange(
                                "c (t r) -> t c r", t=64, r=128),
                            etile[k1 % 2][hh * 64:(hh + 1) * 64,
                                          :].rearrange(
                                "t (c r) -> t c r", c=CH, r=128)
                        ).then_inc(S["escr%d" % (k1 % 2)], 16)

                eng.wait_ge(S["zmm"], p + 1)
                ch_.pre()
                ch_.post(eng.add(lg[:], pb[0][32:36, 0:PR], ebias[:, 0:1]),
                         S["lgc"], p + 1)
                eng.wait_ge(S["t1"], (p + 1) * CH)
                for blk in range(CH):
                    ch_.pre()
                    ch_.post(eng.copy(lgt[:, blk * 4:(blk + 1) * 4],
                                      pb[1][:, blk * 4:(blk + 1) * 4]),
                             S["lgt"], p * CH + blk + 1)

                def et_load(ch):
                    gch2 = p * CH + ch
                    eng.wait_ge(S["escr0"], 16 * (p + 1) * ESCR_PP[0])
                    eng.wait_ge(S["escr1"], 16 * (p + 1) * ESCR_PP[1])
                    if gch2 >= 2:
                        eng.wait_ge(S["iamm"], 8 * (gch2 - 1))
                    src2 = escr_d.rearrange(
                        "(k c) f -> k c f", c=CH)[:, ch:ch + 1,
                                                  :].rearrange(
                        "k o f -> k (o f)")
                    eng.dma_start(et[gch2 % 2][:], src2
                                  ).then_inc(S["et%d" % (gch2 % 2)], 16)

                et_load(0)
                et_load(1)

                for cc in range(CH):
                    gch = p * CH + cc
                    for g in range(8):
                        gi = gch * 8 + g
                        eng.wait_ge(S["iamm"], gi + 1)
                        if g == 0:
                            eng.wait_ge(S["top"], 2 * gch)
                        ch_.pre()
                        ch_.post(eng.copy(corr[:, g * 512:(g + 1) * 512],
                                          pb[4 + g % 4][:]),
                                 S["cev"], gi + 1)
                    if cc + 2 < CH:
                        et_load(cc + 2)
                    eng.wait_ge(S["top"], 2 * gch + 2)
                    if gch >= 2:
                        eng.wait_ge(S["sm"], gch - 1)
                    ch_.pre()
                    ch_.post(eng.activation(e8[cc % 2][:], v8[cc % 2][:],
                                            AF.Exp,
                                            bias=negv0[cc % 2][:, 0:1]),
                             S["exp"], gch + 1)

                eng.wait_ge(S["t3"], (p + 1) * CH)
                for blk in range(CH):
                    ch_.pre()
                    ch_.post(eng.copy(zgs[:, blk * 128:(blk + 1) * 128],
                                      pb[2][0:32, blk * 128:(blk + 1) * 128]),
                             S["zgs"], p * CH + blk + 1)

                for oq in range(4):
                    for rb in range(CH):
                        gi = (p * 4 + oq) * CH + rb
                        eng.wait_ge(S["fev"], gi + 1)
                        rows = slice(pbase + rb * 128, pbase + (rb + 1) * 128)
                        eng.dma_start(out_d[rows, oq * 1024:(oq + 1) * 1024],
                                      outh[gi % 2][:]
                                      ).then_inc(S["out%d" % (gi % 2)], 16)


        @block.vector
        def _(eng):
            ch_ = Chain(eng, S["chv"])
            eng.wait_ge(S["const"], NCONST_DMA)
            ch_.post(eng.memset(pad8[:, 4:8], -3.0e38))
            eng.wait_ge(S["iot"], 1)

            def ln_v_pre(p, cc):
                gch = p * CH + cc
                if gch >= 1:
                    eng.wait_ge(S["act"], 2 * gch)
                for half in range(2):
                    h = cc * 2 + half
                    eng.wait_ge(S["x%d" % (h % 2)],
                                16 * (p * CH + h // 2 + 1))
                    for sgi in range(4):
                        ch_.pre()
                        ch_.post(eng.bn_stats(
                            stat6[:, (half * 4 + sgi) * 6:
                                  (half * 4 + sgi + 1) * 6],
                            xf[h % 2][:, sgi * 512:(sgi + 1) * 512]))
                ch_.pre()
                ch_.post(eng.bn_aggr(mv2[:], stat6[:]))
                ch_.pre()
                ch_.post(eng.tensor_scalar(sv[:], mv2[:, 1:2], 1e-5,
                                           None, op0=ALU.add),
                         S["vpre"], gch + 1)
                ch_.pre()
                ch_.post(eng.tensor_scalar_mul(nmb[:], mv2[:, 0:1], -1.0))

            def ln_v_post(p, cc):
                gch = p * CH + cc
                eng.wait_ge(S["sqrt"], gch + 1)
                ch_.pre()
                ch_.post(eng.reciprocal(rstd[:], sv[:]), S["vstat"], gch + 1)
                for half in range(2):
                    h = cc * 2 + half
                    gi = p * H2 + h
                    if gi >= 2:
                        eng.wait_ge(S["xn2_%d" % (h % 2)],
                                    16 * 2 * (p * CH + h // 2))
                    ch_.pre()
                    ch_.post(eng.tensor_scalar(xnr[h % 2][:],
                                               xf[h % 2][:],
                                               nmb[:, 0:1], rstd[:, 0:1],
                                               op0=ALU.add, op1=ALU.mult),
                             S["act"], gi + 1)

            for cc in range(CH):
                ln_v_pre(0, cc)
                ln_v_post(0, cc)
            for p in range(NP):
                if p + 1 < NP:
                    for cc in range(CH):
                        ln_v_pre(p + 1, cc)
                        ln_v_post(p + 1, cc)

                if p >= 1:
                    eng.wait_ge(S["bmm"], p * 33 - 1)
                ch_.pre()
                ch_.post(eng.memset(mt[0][64:128, :], 0.0),
                         S["mz"], p * 2 + 1)

                if p >= 1:
                    eng.wait_ge(S["fmm"], p * 4 * CH)
                for k1 in range(33):
                    gi = p * 33 + k1
                    eng.wait_ge(S["xev"], gi + 1)
                    ch_.pre()
                    ch_.post(eng.tensor_mul(t1b[k1 % 2][:],
                                            xh[k1 % 2][0:64, :],
                                            xh[k1 % 2][0:64, :]))
                    eng.wait_ge(S["sq"], gi + 1)
                    ch_.pre()
                    ch_.post(eng.tensor_add(p_slot(k1), t1b[k1 % 2][:],
                                            t2b[k1 % 2][:]),
                             S["pev"], gi + 1)
                    if k1 == 30:
                        eng.wait_ge(S["bmm"], p * 33 + 31)
                        ch_.pre()
                        ch_.post(eng.memset(mt[0][64:128, :], 0.0),
                                 S["mz"], p * 2 + 2)

                eng.wait_ge(S["zmm"], p + 1)
                eng.wait_ge(S["lgc"], p + 1)
                ch_.pre()
                ch_.post(eng.tensor_copy(zt[:], pb[0][0:32, 0:PR]),
                         S["zev"], p + 1)

                def v_apply(p2, cc2):
                    par2 = cc2 % 2
                    gch2 = p2 * CH + cc2
                    base_applv = applv_cum(p2, cc2) - 8
                    eng.wait_ge(S["xal%d" % (cc2 % 2)],
                                16 * (CH // 2) * (p2 + 1))
                    dstx = xa[:, cc2 * 4096:(cc2 + 1) * 4096]
                    for j in range(8):
                        eng.wait_ge(S["gath%d" % (j % 2)],
                                    16 * (gch2 * 8 + 2 * (j // 2 + 1)))
                        ch_.pre()
                        ch_.post(eng.tensor_scalar(
                            roll[j % 2][:], roll[j % 2][:],
                            w8[par2][:, j:j + 1], None, op0=ALU.mult))
                        ch_.pre()
                        ch_.post(eng.tensor_tensor(dstx, dstx,
                                                   roll[j % 2][:], ALU.add),
                                 S["applv"], base_applv + j + 1)

                if p >= 1:
                    eng.wait_ge(S["t3"], p * CH)
                eng.wait_ge(S["lgt"], (p + 1) * CH)
                eng.wait_ge(S["t2"], (p + 1) * CH)
                for blk in range(CH):
                    ch_.pre()
                    ch_.post(eng.tensor_copy(
                        pad8[:, 0:4], lgt[:, blk * 4:(blk + 1) * 4]))
                    ch_.pre()
                    ch_.post(eng.max(gv8[:], pad8[:]))
                    ch_.pre()
                    ch_.post(eng.max_index(pidx[:], gv8[:], pad8[:]))
                    ch_.pre()
                    ch_.post(eng.tensor_copy(idxf[:], pidx[:, 0:1]))
                    ch_.pre()
                    ch_.post(eng.tensor_scalar(oh[:], iota4[:],
                                               idxf[:, 0:1],
                                               None,
                                               op0=ALU.is_equal))
                    ch_.pre()
                    ch_.post(eng.tensor_copy(
                        pzt[:, blk * 32:(blk + 1) * 32],
                        pb[1][:, 16 + blk * 32:16 + (blk + 1) * 32]),
                        S["pzt"], p * CH + blk + 1)
                    ch_.pre()
                    ch_.post(eng.tensor_tensor(
                        zgt[:, blk * 32:(blk + 1) * 32].rearrange(
                            "q (e r) -> q e r", e=4, r=8),
                        pzt[:, blk * 32:(blk + 1) * 32].rearrange(
                            "q (e r) -> q e r", e=4, r=8),
                        oh[:].to_broadcast([128, 4, 8]),
                        ALU.mult), S["zgt"], p * CH + blk + 1)

                for cc in range(CH):
                    par = cc % 2
                    gch = p * CH + cc
                    eng.wait_ge(S["cev"], gch * 8 + 8)
                    if gch >= 2:
                        eng.wait_ge(S["gath0"], 16 * 8 * (gch - 1))
                        eng.wait_ge(S["gath1"], 16 * 8 * (gch - 1))
                        appl_done_wait(eng, gch - 2)
                        eng.wait_ge(S["exp"], gch - 1)
                    ch_.pre()
                    ch_.post(eng.max(v8[par][:], corr[:]))
                    ch_.pre()
                    ch_.post(eng.max_index(idx8[par][:], v8[par][:], corr[:]))
                    ch_.pre()
                    ch_.post(eng.tensor_scalar(tt1[:], idx8[par][:], 63, None,
                                               op0=ALU.bitwise_and))
                    ch_.pre()
                    ch_.post(eng.tensor_scalar(tt1[:], tt1[:], 6, None,
                                               op0=ALU.logical_shift_left))
                    ch_.pre()
                    ch_.post(eng.tensor_scalar(tt2[:], idx8[par][:], 6, None,
                                               op0=ALU.logical_shift_right))
                    ch_.pre()
                    ch_.post(eng.tensor_tensor(tau[par][:], tt1[:], tt2[:],
                                               ALU.add))
                    ch_.pre()
                    ch_.post(eng.tensor_tensor(
                        offs[par][:], tau[par][:],
                        rowb[:, cc:cc + 1].to_broadcast(
                            [128, 8]), ALU.add))
                    ch_.pre()
                    ch_.post(eng.tensor_scalar_mul(negv0[par][:],
                                                   v8[par][:, 0:1], -1.0),
                             S["top"], 2 * gch + 2, inc=2)
                    eng.wait_ge(S["exp"], gch + 1)
                    ch_.pre()
                    ch_.post(eng.reduce_sum(s8[:], e8[par][:], axis=AX.X))
                    ch_.pre()
                    ch_.post(eng.reciprocal(r8[:], s8[:]))
                    ch_.pre()
                    ch_.post(eng.tensor_scalar(w8[par][:], e8[par][:],
                                               r8[:, 0:1], None,
                                               op0=ALU.mult),
                             S["sm"], gch + 1)
                    if cc >= 1:
                        v_apply(p, cc - 1)
                v_apply(p, CH - 1)

                for oq in range(4):
                    for rb in range(CH):
                        gi = (p * 4 + oq) * CH + rb
                        eng.wait_ge(S["fmm"], gi + 1)
                        appl_done_wait(eng, p * CH + rb)
                        if gi >= 2:
                            eng.wait_ge(S["out%d" % (gi % 2)],
                                        16 * (gi // 2))
                        for bk in range(2):
                            ch_.pre()
                            ins = eng.tensor_tensor(
                                outh[gi % 2][:, bk * 512:(bk + 1) * 512],
                                pb[2 * rb + bk][:],
                                xa[:, rb * 4096 + oq * 1024 + bk * 512:
                                   rb * 4096 + oq * 1024 + (bk + 1) * 512],
                                ALU.add)
                            if bk == 1:
                                ch_.post(ins, S["fev"], gi + 1)
                            else:
                                ch_.post(ins)


        @block.gpsimd
        def _(eng):
            ch_ = Chain(eng, S["chg"])
            eng.dma_start(agb[:].rearrange("p (c a) -> p c a", c=32, a=36),
                          ag_d.rearrange("(c p) a -> p c a", p=128)
                          ).then_inc(S["cg"], 16)
            eng.dma_start(bcatb[:], bcat_d).then_inc(S["cg"], 16)
            for ci in range(CH):
                ch_.pre()
                ins = eng.iota(rowb[:, ci:ci + 1], [[0, 1]],
                               base=ci * 128 * 8192, channel_multiplier=8192)
                if ci == CH - 1:
                    ch_.post(ins, S["iot"], 1)
                else:
                    ch_.post(ins)

            for wi in range(NW):
                oq, fc = divmod(wi, 32)
                eng.dma_start(
                    wscr_d[fc * 128:(fc + 1) * 128,
                           oq * 1024:(oq + 1) * 1024],
                    wt_d[fc * 128:(fc + 1) * 128,
                         oq * 1024:(oq + 1) * 1024]
                ).then_inc(S["wscr%d" % (wi % 2)], 16)

            for p in range(NP):
                xn2_flat = xn2_d[p].rearrange("(o a) b -> o (a b)", o=1)
                for cc in range(CH):
                    if p >= 1:
                        eng.wait_ge(S["fev"],
                                    (p - 1) * 4 * CH + 3 * CH + cc + 1)
                    eng.dma_start(
                        xa[:, cc * 4096:(cc + 1) * 4096],
                        x_d[p * PR + cc * 128:p * PR + (cc + 1) * 128, :]
                    ).then_inc(S["xal%d" % (cc % 2)], 16)

                for cc in range(CH):
                    par = cc % 2
                    gch = p * CH + cc
                    eng.wait_ge(S["sm"], gch + 1)
                    eng.wait_ge(S["xn2_0"], 16 * 2 * CH * (p + 1))
                    eng.wait_ge(S["xn2_1"], 16 * 2 * CH * (p + 1))
                    for j in range(8):
                        roll_free_wait(eng, gch * 8 + j)
                        for hf in range(2):
                            eng.indirect_dma_start(
                                roll[j % 2][:, hf * 2048:(hf + 1) * 2048],
                                None, xn2_flat,
                                IndirectOffsetOnAxis(offs[par][:, j:j + 1],
                                                     1),
                                element_offset=hf * 2048,
                            ).then_inc(S["gath%d" % (j % 2)], 16)

                for oq in range(4):
                    for fc in range(32):
                        wi = p * NW + oq * 32 + fc
                        if wi % 2 == 0:
                            continue
                        if wi == 1:
                            eng.wait_ge(S["wscr0"], 16 * (NW // 2))
                            eng.wait_ge(S["wscr1"], 16 * (NW // 2))
                        if wi >= 2:
                            eng.wait_ge(S["wcons"], wi - 1)
                        eng.dma_start(
                            wtile[1][:],
                            wscr_d[fc * 128:(fc + 1) * 128,
                                   oq * 1024:(oq + 1) * 1024]
                        ).then_inc(S["w1"], 16)


        @block.tensor
        def _(eng):
            eng.wait_ge(S["const"], NCONST_DMA)
            eng.wait_ge(S["cg"], 32)
            for p in range(NP):
                for cp in range(PAIRS):
                    for t in range(16):
                        gi = (p * PAIRS + cp) * 16 + t
                        if gi == p * PAIRS * 16 and p >= 1:
                            eng.wait_ge(S["fev"], p * 4 * CH - CH + 1)
                        if gi == p * PAIRS * 16 + 2 and p >= 1:
                            eng.wait_ge(S["fev"], p * 4 * CH - 1)
                        q = t // 4
                        eng.wait_ge(S["comb%d" % (q % 2)],
                                    16 * 2 * (p * PAIRS * 2 + cp * 2 +
                                              q // 2 + 1))
                        if gi >= 4:
                            eng.wait_ge(S["aev"], gi - 3)
                        mv = xat[q % 2][:, (t % 4) * 512:(t % 4 + 1) * 512]
                        eng.matmul(pb[(0, 1, 4, 5)[t % 4]][:], gab[:], mv,
                                   start=True, stop=True).then_inc(S["amm"], 1)

                for k1 in range(33):
                    gi = p * 33 + k1
                    if k1 == 0 and p >= 1:
                        eng.wait_ge(S["fev"], p * 4 * CH)
                    if k1 == 2 and p >= 1:
                        eng.wait_ge(S["cev"], 8 * CH * p)
                    eng.wait_ge(S["sbt%d" % (k1 % 2)],
                                16 * (p * SBT_PP[k1 % 2] + k1 // 2 + 1))
                    eng.wait_ge(S["bmov%d" % (k1 % 2)],
                                16 * (p * BMOV_PP[k1 % 2] + bmov_pcum[k1]))
                    if k1 in (0, 32):
                        eng.wait_ge(S["mz"], p * 2 + (1 if k1 == 0 else 2))
                    if gi >= 4:
                        eng.wait_ge(S["xev"], gi - 3)
                    eng.matmul(pb[(2, 3, 6, 7)[k1 % 4]][:, 0:PR],
                               sbtb[k1 % 2][:],
                               mt[k1 % 2][:], start=True, stop=True
                               ).then_inc(S["bmm"], 1)

                for k1 in range(33):
                    gi = p * 33 + k1
                    if k1 == 0 and p >= 1:
                        eng.wait_ge(S["cev"], 8 * CH * p)
                    eng.wait_ge(S["pev"], gi + 1)
                    if gi >= 2:
                        eng.wait_ge(S["eev"], gi - 1)
                    sib_h = sibb[64 * (k1 % 2):64 * (k1 % 2) + 64, :]
                    eng.matmul(pb[4 + k1 % 2][:, 0:PR], sib_h, p_slot(k1),
                               start=True, stop=True).then_inc(S["ibmm"], 1)

                eng.wait_ge(S["aev"], 16 * PAIRS * (p + 1))
                eng.wait_ge(S["xev"], 33 * (p + 1))
                eng.wait_ge(S["eev"], 33 * (p + 1))
                eng.wait_ge(S["xnfr"], 16 * 32 * (p + 1))
                for fc in range(32):
                    ins = eng.matmul(pb[0][0:36, 0:PR],
                                     agb[:, fc * 36:(fc + 1) * 36],
                                     arena[:, fc * PR:(fc + 1) * PR],
                                     start=(fc == 0), stop=(fc == 31))
                ins.then_inc(S["zmm"], 1)
                eng.wait_ge(S["lgc"], p + 1)
                for blk in range(CH):
                    eng.transpose(pb[1][:, blk * 4:(blk + 1) * 4],
                                  lg[:, blk * 128:(blk + 1) * 128],
                                  identf[0:4, 0:4]).then_inc(S["t1"], 1)
                eng.wait_ge(S["zev"], p + 1)
                eng.wait_ge(S["lgt"], (p + 1) * CH)
                for blk in range(CH):
                    eng.transpose(
                        pb[1][:, 16 + blk * 32:16 + (blk + 1) * 32],
                        zt[:, blk * 128:(blk + 1) * 128],
                        identf[0:32, 0:32]).then_inc(S["t2"], 1)

                for ch in range(CH):
                    gch = p * CH + ch
                    eng.wait_ge(S["et%d" % (gch % 2)],
                                16 * (gch // 2 + 1))
                    for g in range(8):
                        gi = (p * CH + ch) * 8 + g
                        if g == 0 and ch == 0:
                            pass
                        if gi >= 4:
                            eng.wait_ge(S["cev"], gi - 3)
                        for tl8 in range(8):
                            tl = g * 8 + tl8
                            ins = eng.matmul(
                                pb[4 + g % 4][:, tl8 * 64:(tl8 + 1) * 64],
                                et[gch % 2][:, tl * 128:(tl + 1) * 128],
                                miab[:, tl * 64:(tl + 1) * 64],
                                start=True, stop=True)
                        ins.then_inc(S["iamm"], 1)

                for blk in range(CH):
                    eng.wait_ge(S["zgt"], p * CH + blk + 1)
                    eng.transpose(pb[2][0:32, blk * 128:(blk + 1) * 128],
                                  zgt[:, blk * 32:(blk + 1) * 32],
                                  identf[:]).then_inc(S["t3"], 1)

                for oq in range(4):
                    for fc in range(32):
                        wi = p * NW + oq * 32 + fc
                        idx = oq * 32 + fc
                        eng.wait_ge(S["w%d" % (wi % 2)],
                                    16 * (wi // 2 + 1))
                        if fc == 0:
                            if oq == 0:
                                eng.wait_ge(S["cev"], (p + 1) * CH * 8)
                                eng.wait_ge(S["lgt"], (p + 1) * CH)
                                eng.wait_ge(S["pzt"], (p + 1) * CH)
                                eng.wait_ge(S["zgs"], (p + 1) * CH)
                        for rb in range(CH):
                            if fc == 0 and oq >= 1:
                                eng.wait_ge(S["fev"],
                                            p * 4 * CH + (oq - 1) * CH +
                                            rb + 1)
                            lhs = arena[:, fc * PR + rb * 128:
                                        fc * PR + (rb + 1) * 128]
                            for bk in range(2):
                                ins = eng.matmul(
                                    pb[2 * rb + bk][:], lhs,
                                    wtile[wi % 2][:, bk * 512:(bk + 1) * 512],
                                    start=(fc == 0), stop=False)
                        ins.then_inc(S["wcons"], 1)
                    for rb in range(CH):
                        lhs = zgs[:, rb * 128:(rb + 1) * 128]
                        for bk in range(2):
                            ins = eng.matmul(
                                pb[2 * rb + bk][:], lhs,
                                bcatb[:, oq * 1024 + bk * 512:
                                      oq * 1024 + (bk + 1) * 512],
                                start=False, stop=True)
                        ins.then_inc(S["fmm"], 1)

    es.close()
    return nc


def kernel(L0_attn_x, L0_ffn_W_up, L0_ffn_gate_w, L0_ffn_expert_bias,
           L0_ffn_lora_A, L0_ffn_lora_B):
    x = np.ascontiguousarray(np.asarray(L0_attn_x).reshape(-1, L))
    wt = np.ascontiguousarray(np.asarray(L0_ffn_W_up).T)
    ag = np.ascontiguousarray(np.concatenate(
        [np.asarray(L0_ffn_lora_A).reshape(32, L),
         np.asarray(L0_ffn_gate_w)], axis=0).T)
    bcat = np.ascontiguousarray(
        np.asarray(L0_ffn_lora_B).transpose(0, 2, 1).reshape(32, L))
    eb = np.ascontiguousarray(
        np.asarray(L0_ffn_expert_bias).reshape(4, 1).astype(np.float32,
                                                            copy=False))
    tabs = _host_tables()

    cfg = Cfg(NP=4, CH=4)
    nc = build_nc(cfg)
    shards = np.split(x, NCORES, axis=0)
    in_maps = []
    for i in range(NCORES):
        m = {"x": np.ascontiguousarray(shards[i]), "wt": wt, "ag": ag,
             "bcat": bcat, "ebias": eb}
        m.update(tabs)
        in_maps.append(m)
    res = run_bass_kernel_spmd(nc, in_maps, core_ids=list(range(NCORES)))
    out = np.concatenate([np.asarray(r["out"]) for r in res.results], axis=0)
    return out.reshape(4, L, L).astype(np.float32, copy=False)


# version 23 (final)
# speedup vs baseline: 1.3305x; 1.3305x over previous
import numpy as np
import ml_dtypes
from contextlib import ExitStack

import concourse.bass as bass
import concourse.mybir as mybir
from concourse.bass_utils import run_bass_kernel_spmd
from concourse.bass import IndirectOffsetOnAxis

F32 = mybir.dt.float32
BF16 = mybir.dt.bfloat16
U32 = mybir.dt.uint32
AF = mybir.ActivationFunctionType
ALU = mybir.AluOpType
AX = mybir.AxisListType
BF = ml_dtypes.bfloat16

L = 4096
NCORES = 8


def _host_tables():
    t = {}
    n1 = np.arange(64)
    G = np.zeros((64, 64), np.float64)
    for q in range(33):
        G[:, q] = np.cos(2 * np.pi * n1 * q / 64.0)
    for k1 in range(1, 32):
        G[:, 32 + k1] = -np.sin(2 * np.pi * n1 * k1 / 64.0)
    GA = np.zeros((128, 128), np.float64)
    GA[:64, :64] = G
    GA[64:, 64:] = G
    t["ga"] = GA.astype(np.float32).astype(BF)

    n2 = np.arange(64)
    SB = np.zeros((33, 128, 128), np.float64)
    for q in range(33):
        k = q + 64 * np.arange(64)
        th = 2 * np.pi * np.outer(n2, k) / L
        SB[q, :64, :64] = np.cos(th)
        SB[q, 64:, :64] = np.sin(th)
        SB[q, :64, 64:] = -np.sin(th)
        SB[q, 64:, 64:] = np.cos(th)
    t["sbt"] = SB.reshape(33 * 128, 128).astype(np.float32).astype(BF)

    k2 = np.arange(64)
    tlo = np.arange(64)
    ang = 2 * np.pi * np.outer(k2, tlo) / 64.0
    SIB = np.zeros((64, 128), np.float64)
    SIB[:, :64] = np.cos(ang)
    SIB[:, 64:] = np.sin(ang)
    t["sib"] = SIB.astype(np.float32).astype(BF)

    m = np.full(33, 2.0)
    m[0] = 1.0
    m[32] = 1.0
    k1v = np.arange(33)
    thi = np.arange(64)
    MIA = np.zeros((66, 64 * 64), np.float64)
    for tl in range(64):
        phi = 2 * np.pi * np.outer(k1v, tl + 64 * thi) / L
        MIA[:33, tl * 64:(tl + 1) * 64] = (m[:, None] / L) * np.cos(phi)
        MIA[33:, tl * 64:(tl + 1) * 64] = -(m[:, None] / L) * np.sin(phi)
    t["mia"] = MIA.astype(np.float32).astype(BF)

    t["identf"] = np.eye(128, dtype=np.float32)
    t["iota4"] = np.tile(np.arange(4, dtype=np.float32), (128, 1))
    t["iota4"] = np.ascontiguousarray(t["iota4"])
    return t


class Cfg:
    def __init__(self, NP=4, CH=4):
        self.NP = NP
        self.CH = CH
        assert CH % 2 == 0 and CH <= 4
        self.PAIRS = CH // 2
        self.R = NP * CH * 128


def build_nc(cfg):
    NP, CH, PAIRS, R = cfg.NP, cfg.CH, cfg.PAIRS, cfg.R
    PR = CH * 128
    H2 = 2 * CH

    nc = bass.Bass("TRN2", target_bir_lowering=False)

    x_d = nc.dram_tensor("x", [R, L], F32, kind="ExternalInput").ap()
    wt_d = nc.dram_tensor("wt", [L, L], F32, kind="ExternalInput").ap()
    ag_d = nc.dram_tensor("ag", [L, 36], F32, kind="ExternalInput").ap()
    bcat_d = nc.dram_tensor("bcat", [32, L], F32, kind="ExternalInput").ap()
    ebias_d = nc.dram_tensor("ebias", [4, 1], F32, kind="ExternalInput").ap()
    ga_d = nc.dram_tensor("ga", [128, 128], BF16, kind="ExternalInput").ap()
    sbt_d = nc.dram_tensor("sbt", [33 * 128, 128], BF16,
                           kind="ExternalInput").ap()
    sib_d = nc.dram_tensor("sib", [64, 128], BF16, kind="ExternalInput").ap()
    mia_d = nc.dram_tensor("mia", [66, 4096], BF16, kind="ExternalInput").ap()
    identf_d = nc.dram_tensor("identf", [128, 128], F32,
                              kind="ExternalInput").ap()
    iota4_d = nc.dram_tensor("iota4", [128, 4], F32, kind="ExternalInput").ap()
    out_d = nc.dram_tensor("out", [R, L], F32, kind="ExternalOutput").ap()

    xn2_d = [nc.dram_tensor("xn2_%d" % p, [CH * 128, 8192], BF16,
                            kind="Internal").ap() for p in range(NP)]
    xn2t_d = [nc.dram_tensor("xn2t_%d" % p, [L, CH * 128], BF16,
                             kind="Internal").ap() for p in range(NP)]
    ascr_d = nc.dram_tensor("ascr", [128, PAIRS * 8192], BF16,
                            kind="Internal").ap()
    escr_d = nc.dram_tensor("escr", [66 * CH, 64 * 128], BF16,
                            kind="Internal").ap()
    wscr_d = nc.dram_tensor("wscr", [L, L], BF16, kind="Internal").ap()

    N_COMB = PAIRS * 4
    N_BMOV = 64
    N_XN2 = 2 * H2
    N_ASCR = PAIRS * 4
    N_OUT = 4 * CH
    NW = 128

    bmov_pcum = []
    pc = [0, 0]
    for k1 in range(33):
        pc[k1 % 2] += 2 + (0 if k1 in (0, 32) else 2)
        bmov_pcum.append(pc[k1 % 2])
    BMOV_PP = (pc[0], pc[1])
    SBT_PP = (17, 16)
    ESCR_PP = (34, 32)

    def apply_on_v(cc):
        return True

    def applv_cum(p, cc):
        return (p * CH + cc + 1) * 8

    def appl_done_wait(eng, gch):
        if gch < 0:
            return
        eng.wait_ge(S["applv"], (gch + 1) * 8)

    def roll_free_wait(eng, gk):
        prev = gk - 2
        if prev < 0:
            return
        eng.wait_ge(S["applv"], prev + 1)

    sem_names = [
        "x0", "x1", "comb0", "comb1", "sbt0", "sbt1", "bmov0", "bmov1",
        "et0", "et1", "xnfr", "w0", "w1", "w2", "w3", "wg0", "wg1",
        "xn2_0", "xn2_1",
        "ascr0", "ascr1",
        "escr0", "escr1", "out0", "out1", "wscr0", "wscr1",
        "gath0", "gath1",
        "vpre", "sqrt", "vstat", "act", "xres", "mz",
        "amm", "aev", "bmm", "xev", "sq", "pev", "ibmm", "eev",
        "iamm", "cev", "top", "exp", "sm", "applv", "applg", "fev",
        "zmm", "lgc", "zev", "t1", "lgt", "t2", "pzt", "zgt",
        "t3", "zgs", "fmm", "wcons", "iot", "const",
        "chv", "chs", "chg", "cg", "xal0", "xal1",
        "xbt0", "xbt1", "xtw0", "xtw1", "mul",
    ]
    es = ExitStack()
    S = {n: es.enter_context(nc.semaphore(name="s_" + n)) for n in sem_names}

    def sb(name, shape, dtype, **kw):
        return es.enter_context(nc.sbuf_tensor(name, shape, dtype, **kw))

    def psum(name, shape, dtype):
        return es.enter_context(nc.psum_tensor(name, shape, dtype))

    xf = [sb("xf%d" % i, [128, 2048], F32) for i in range(2)]
    xnr = [sb("xnr%d" % i, [128, 2048], BF16) for i in range(2)]
    xa = sb("xa", [128, CH * 4096], BF16)
    xat = [sb("xat%d" % i, [128, 2048], BF16) for i in range(2)]
    abuf = sb("abuf", [128, 2048], BF16)
    mt = [sb("mt%d" % i, [128, PR], BF16) for i in range(2)]
    xh = [sb("xh%d" % i, [128, PR], BF16) for i in range(2)]
    t1b = [sb("t1b", [64, PR], BF16)] * 2
    t2b = [sb("t2b%d" % i, [64, PR], BF16) for i in range(2)]
    arena = sb("arena", [128, 32 * PR], BF16, is_transpose_dest=True)
    etile = [sb("etile%d" % i, [128, PR], BF16) for i in range(2)]
    et = [sb("et%d" % i, [66, 8192], BF16) for i in range(2)]
    tmpt = [sb("tmpt%d" % i, [128, PR], BF16, is_transpose_dest=True)
            for i in range(2)]
    corr = sb("corr", [128, 4096], BF16)
    roll = [sb("roll%d" % i, [128, 4096], BF16) for i in range(2)]
    wtile = [sb("wtile%d" % i, [128, 1024], BF16) for i in range(3)]
    outh = [sb("outh%d" % i, [128, 1024], F32) for i in range(2)]
    agb = sb("agb", [128, 32 * 36], BF16)
    bcatb = sb("bcatb", [32, 4096], BF16)
    ebias = sb("ebias_s", [4, 1], F32)
    gab = sb("gab", [128, 128], BF16)
    sbtb = [sb("sbtb%d" % i, [128, 128], BF16) for i in range(2)]
    sibb = sb("sibb", [128, 128], BF16)
    miab = sb("miab", [66, 4096], BF16)
    identf = sb("identf_s", [128, 128], F32)
    iota4 = sb("iota4_s", [128, 4], F32)
    rowb = sb("rowb", [128, CH], U32)
    stat6 = sb("stat6", [128, 48], F32)
    mv2 = sb("mv2", [128, 2], F32)
    sv = sb("sv", [128, 1], F32)
    rstd = sb("rstd", [128, 1], F32)
    nmb = sb("nmb", [128, 1], F32)
    v8 = [sb("v8_%d" % i, [128, 8], F32) for i in range(2)]
    idx8 = [sb("idx8_%d" % i, [128, 8], U32) for i in range(2)]
    tau = [sb("tau%d" % i, [128, 8], U32) for i in range(2)]
    tt1 = sb("tt1", [128, 8], U32)
    tt2 = sb("tt2", [128, 8], U32)
    offs = [sb("offs%d" % i, [128, 8], U32) for i in range(2)]
    negv0 = [sb("negv0_%d" % i, [128, 1], F32) for i in range(2)]
    e8 = [sb("e8_%d" % i, [128, 8], F32) for i in range(2)]
    s8 = sb("s8", [128, 1], F32)
    r8 = sb("r8", [128, 1], F32)
    w8 = [sb("w8_%d" % i, [128, 8], F32) for i in range(2)]
    zt = sb("zt", [32, PR], F32)
    lg = sb("lg", [4, PR], F32)
    lgt = sb("lgt", [128, 4 * CH], F32)
    pad8 = sb("pad8", [128, 8], F32)
    gv8 = sb("gv8", [128, 8], F32)
    pidx = sb("pidx", [128, 8], U32)
    idxf = sb("idxf", [128, 1], F32)
    oh = sb("oh", [128, 4], BF16)
    pzt = sb("pzt", [128, 32 * CH], BF16)
    zgt = sb("zgt", [128, 32 * CH], F32)
    zgs = sb("zgs", [32, PR], BF16)

    pb = [psum("pb%d" % i, [128, 512], F32) for i in range(8)]

    def p_slot(k1):
        base = 64 * (k1 % 2)
        col = (k1 // 2) * PR
        return arena[base:base + 64, col:col + PR]

    NCONST_DMA = 7 * 16


    class Chain:
        def __init__(self, eng, sem):
            self.e, self.sem, self.n = eng, sem, 0
            self.last = None

        def pre(self):
            if self.last is not None:
                self.e.wait_ge(*self.last)

        def post(self, ins, sem=None, count=None, inc=1):
            if sem is None:
                self.n += 1
                ins.then_inc(self.sem, 1)
                self.last = (self.sem, self.n)
            else:
                ins.then_inc(sem, inc)
                self.last = (sem, count)
            return ins

    with nc.Block() as block:

        @block.sync
        def _(eng):
            eng.dma_start(gab[:], ga_d).then_inc(S["const"], 16)
            eng.dma_start(sibb[0:64, :], sib_d).then_inc(S["const"], 16)
            eng.dma_start(sibb[64:128, :], sib_d).then_inc(S["const"], 16)
            eng.dma_start(miab[:], mia_d).then_inc(S["const"], 16)
            eng.dma_start(identf[:], identf_d).then_inc(S["const"], 16)
            eng.dma_start(iota4[:], iota4_d).then_inc(S["const"], 16)
            eng.dma_start(ebias[:], ebias_d).then_inc(S["const"], 16)


            def x_load_s(p, cc):
                pbase2 = p * PR
                for half in range(2):
                    h = cc * 2 + half
                    gi = p * H2 + h
                    if gi >= 2:
                        pi = gi - 2
                        eng.wait_ge(S["vstat"], pi // 2 + 1)
                        eng.wait_ge(S["act"], pi + 1)
                    rows = x_d[pbase2 + cc * 128: pbase2 + (cc + 1) * 128,
                               half * 2048:(half + 1) * 2048]
                    eng.dma_start(xf[h % 2][:], rows
                                  ).then_inc(S["x%d" % (h % 2)], 16)

            def xbar_chain(pp):
                    eng.wait_ge(S["xn2_0"], 16 * 2 * CH * (pp + 1))
                    eng.wait_ge(S["xn2_1"], 16 * 2 * CH * (pp + 1))
                    for fc in range(32):
                        gi = pp * 32 + fc
                        if gi >= 2:
                            eng.wait_ge(S["xtw%d" % (fc % 2)],
                                            16 * (pp * 16 + fc // 2))
                        eng.dma_start(tmpt[fc % 2][:],
                                          xn2_d[pp][0:PR, fc * 128:(fc + 1) * 128],
                                          transpose=True
                                          ).then_inc(S["xbt%d" % (fc % 2)], 16)
                        eng.wait_ge(S["xbt%d" % (fc % 2)],
                                        16 * (pp * 16 + fc // 2 + 1))
                        eng.dma_start(xn2t_d[pp][fc * 128:(fc + 1) * 128, :],
                                          tmpt[fc % 2][:]
                                          ).then_inc(S["xtw%d" % (fc % 2)], 16)

            x_load_s(0, 0)
            X0_DONE = True
            for p in range(NP):
                pbase = p * PR
                if p == 0:
                    for cc in range(1, CH):
                        x_load_s(0, cc)
                if p + 1 < NP:
                    x_load_s(p + 1, 0)
                if p == 0:
                    xbar_chain(0)

                eng.wait_ge(S["xtw0"], 16 * 16 * (p + 1))
                eng.wait_ge(S["xtw1"], 16 * 16 * (p + 1))
                tv = xn2t_d[p].rearrange("(n v) r -> n v r", n=64, v=64)
                for cp in range(PAIRS):
                    for q in range(4):
                        lq = cp * 4 + q
                        if (p + 1 < NP and lq % 2 == 0 and 0 < lq // 2 < CH):
                            x_load_s(p + 1, lq // 2)
                        qi = (p * PAIRS + cp) * 4 + q
                        if qi >= 2:
                            eng.wait_ge(S["amm"], 4 * (qi - 1))
                        for g in range(2):
                            rs = cp * 256 + g * 128
                            src = tv[:, 16 * q:16 * (q + 1), rs:rs + 128]
                            dst = xat[q % 2][64 * g:64 * (g + 1), :].rearrange(
                                "n (v r) -> n v r", v=16, r=128)
                            eng.dma_start(dst, src).then_inc(
                                S["comb%d" % (q % 2)], 16)

                av = ascr_d.rearrange("(g kp) (cp v r) -> kp v cp g r",
                                      g=2, cp=PAIRS, v=64, r=128)
                for k1 in range(33):
                    gi = p * 33 + k1
                    if gi >= 2:
                        eng.wait_ge(S["bmm"], gi - 1)
                    eng.dma_start(sbtb[k1 % 2][:],
                                  sbt_d[k1 * 128:(k1 + 1) * 128, :]
                                  ).then_inc(S["sbt%d" % (k1 % 2)], 16)
                    eng.wait_ge(S["ascr0"], 16 * (p + 1) * (N_ASCR // 2))
                    eng.wait_ge(S["ascr1"], 16 * (p + 1) * (N_ASCR // 2))
                    mtv0 = mt[k1 % 2][0:64, :].rearrange(
                        "v (cp g r) -> v cp g r", cp=PAIRS, g=2, r=128)
                    for g in range(2):
                        eng.dma_start(mtv0[:, :, g, :], av[k1][:, :, g, :]
                                      ).then_inc(S["bmov%d" % (k1 % 2)], 16)
                    if k1 not in (0, 32):
                        mtv1 = mt[k1 % 2][64:128, :].rearrange(
                            "v (cp g r) -> v cp g r", cp=PAIRS, g=2, r=128)
                        for g in range(2):
                            eng.dma_start(mtv1[:, :, g, :],
                                          av[32 + k1][:, :, g, :]
                                          ).then_inc(S["bmov%d" % (k1 % 2)],
                                                     16)

                eng.wait_ge(S["ibmm"], 33 * (p + 1))
                if p >= 1:
                    eng.wait_ge(S["fmm"], p * 4 * CH)
                for fc in range(32):
                    src = xn2t_d[p][fc * 128:(fc + 1) * 128, :]
                    eng.dma_start(arena[:, fc * PR:(fc + 1) * PR], src
                                  ).then_inc(S["xnfr"], 16)

                if p + 1 < NP:
                    xbar_chain(p + 1)
                for oq in range(4):
                    for fc in range(32):
                        wi = p * NW + oq * 32 + fc
                        idx = oq * 32 + fc
                        if idx == 0:
                            eng.wait_ge(S["wscr0"], 16 * (NW // 2))
                            eng.wait_ge(S["wscr1"], 16 * (NW // 2))
                        if wi >= 3:
                            eng.wait_ge(S["wcons"], wi - 2)
                        eng.dma_start(
                            wtile[wi % 3][:],
                            wscr_d[fc * 128:(fc + 1) * 128,
                                   oq * 1024:(oq + 1) * 1024]
                        ).then_inc(S["w%d" % (wi % 3)], 16)

            eng.wait_ge(S["out0"], 16 * NP * (N_OUT // 2))
            eng.wait_ge(S["out1"], 16 * NP * (N_OUT // 2))

        @block.scalar
        def _(eng):
            ch_ = Chain(eng, S["chs"])
            eng.wait_ge(S["const"], NCONST_DMA)

            def ln_s(p, cc):
                eng.wait_ge(S["vpre"], p * CH + cc + 1)
                ch_.pre()
                ch_.post(eng.sqrt(sv[:], sv[:]), S["sqrt"], p * CH + cc + 1)
                for half in range(2):
                    h = cc * 2 + half
                    gi = p * H2 + h
                    eng.wait_ge(S["act"], gi + 1)
                    rows = slice(cc * 128, (cc + 1) * 128)
                    eng.dma_start(
                        xn2_d[p][rows, half * 2048:(half + 1) * 2048],
                        xnr[h % 2][:]).then_inc(S["xn2_%d" % (h % 2)], 16)
                    eng.dma_start(
                        xn2_d[p][rows, 4096 + half * 2048:
                              4096 + (half + 1) * 2048],
                        xnr[h % 2][:]).then_inc(S["xn2_%d" % (h % 2)], 16)

            for cc in range(CH):
                ln_s(0, cc)
            for p in range(NP):
                pbase = p * PR

                if p >= 1:
                    eng.wait_ge(S["bmov0"], 16 * p * BMOV_PP[0])
                    eng.wait_ge(S["bmov1"], 16 * p * BMOV_PP[1])
                for cp in range(PAIRS):
                    for t in range(16):
                        lt = cp * 16 + t
                        if p + 1 < NP and lt % 8 == 0 and lt // 8 < CH:
                            ln_s(p + 1, lt // 8)
                        gi = (p * PAIRS + cp) * 16 + t
                        eng.wait_ge(S["amm"], gi + 1)
                        qq = t // 4
                        grp = (p * PAIRS + cp) * 4 + qq
                        if grp >= 1 and t % 4 == 0:
                            eng.wait_ge(S["ascr%d" % ((grp - 1) % 2)],
                                        16 * ((grp - 1) // 2 + 1))
                        dst = abuf[:, (t % 4) * 512:(t % 4 + 1) * 512]
                        ch_.pre()
                        ch_.post(eng.copy(dst, pb[(0, 1, 4, 5)[t % 4]][:]),
                                 S["aev"], gi + 1)
                        if t % 4 == 3:
                            ov = ascr_d[:, cp * 8192 + qq * 2048:
                                        cp * 8192 + (qq + 1) * 2048]
                            ch_.pre()
                            eng.dma_start(ov, abuf[:]
                                          ).then_inc(
                                              S["ascr%d" % (grp % 2)], 16)

                for k1 in range(33):
                    gi = p * 33 + k1
                    eng.wait_ge(S["bmm"], gi + 1)
                    if gi >= 2:
                        eng.wait_ge(S["pev"], gi - 1)
                    ch_.pre()
                    ch_.post(eng.copy(xh[k1 % 2][:],
                                      pb[(2, 3, 6, 7)[k1 % 4]][:, 0:PR]),
                             S["xev"], gi + 1)
                    ch_.pre()
                    ch_.post(eng.square(t2b[k1 % 2][:],
                                        xh[k1 % 2][64:128, :]),
                             S["sq"], gi + 1)

                if p >= 1:
                    eng.wait_ge(S["et0"], 16 * (p * CH // 2))
                    eng.wait_ge(S["et1"], 16 * (p * CH // 2))
                for k1 in range(33):
                    gi = p * 33 + k1
                    eng.wait_ge(S["ibmm"], gi + 1)
                    if gi >= 2:
                        eng.wait_ge(S["escr%d" % (k1 % 2)],
                                    16 * (p * ESCR_PP[k1 % 2] +
                                          2 * (k1 // 2)))
                    ch_.pre()
                    ch_.post(eng.copy(etile[k1 % 2][:],
                                      pb[4 + k1 % 2][:, 0:PR]),
                             S["eev"], gi + 1)
                    ch_.pre()
                    for hh in range(2):
                        kh = hh * 33 + k1
                        eng.dma_start(
                            escr_d[kh * CH:(kh + 1) * CH, :].rearrange(
                                "c (t r) -> t c r", t=64, r=128),
                            etile[k1 % 2][hh * 64:(hh + 1) * 64,
                                          :].rearrange(
                                "t (c r) -> t c r", c=CH, r=128)
                        ).then_inc(S["escr%d" % (k1 % 2)], 16)

                eng.wait_ge(S["zmm"], p + 1)
                ch_.pre()
                ch_.post(eng.add(lg[:], pb[0][32:36, 0:PR], ebias[:, 0:1]),
                         S["lgc"], p + 1)
                eng.wait_ge(S["t1"], (p + 1) * CH)
                for blk in range(CH):
                    ch_.pre()
                    ch_.post(eng.copy(lgt[:, blk * 4:(blk + 1) * 4],
                                      pb[1][:, blk * 4:(blk + 1) * 4]),
                             S["lgt"], p * CH + blk + 1)

                def et_load(ch):
                    gch2 = p * CH + ch
                    eng.wait_ge(S["escr0"], 16 * (p + 1) * ESCR_PP[0])
                    eng.wait_ge(S["escr1"], 16 * (p + 1) * ESCR_PP[1])
                    if gch2 >= 2:
                        eng.wait_ge(S["iamm"], 8 * (gch2 - 1))
                    src2 = escr_d.rearrange(
                        "(k c) f -> k c f", c=CH)[:, ch:ch + 1,
                                                  :].rearrange(
                        "k o f -> k (o f)")
                    eng.dma_start(et[gch2 % 2][:], src2
                                  ).then_inc(S["et%d" % (gch2 % 2)], 16)

                et_load(0)
                et_load(1)

                for cc in range(CH):
                    gch = p * CH + cc
                    for g in range(8):
                        gi = gch * 8 + g
                        eng.wait_ge(S["iamm"], gi + 1)
                        if g == 0:
                            eng.wait_ge(S["top"], 2 * gch)
                        ch_.pre()
                        ch_.post(eng.copy(corr[:, g * 512:(g + 1) * 512],
                                          pb[4 + g % 4][:]),
                                 S["cev"], gi + 1)
                    if cc + 2 < CH:
                        et_load(cc + 2)
                    eng.wait_ge(S["top"], 2 * gch + 2)
                    if gch >= 2:
                        eng.wait_ge(S["sm"], gch - 1)
                    ch_.pre()
                    ch_.post(eng.activation(e8[cc % 2][:], v8[cc % 2][:],
                                            AF.Exp,
                                            bias=negv0[cc % 2][:, 0:1]),
                             S["exp"], gch + 1)

                eng.wait_ge(S["t3"], (p + 1) * CH)
                for blk in range(CH):
                    ch_.pre()
                    ch_.post(eng.copy(zgs[:, blk * 128:(blk + 1) * 128],
                                      pb[2][0:32, blk * 128:(blk + 1) * 128]),
                             S["zgs"], p * CH + blk + 1)

                for oq in range(4):
                    for rb in range(CH):
                        gi = (p * 4 + oq) * CH + rb
                        eng.wait_ge(S["fev"], gi + 1)
                        rows = slice(pbase + rb * 128, pbase + (rb + 1) * 128)
                        eng.dma_start(out_d[rows, oq * 1024:(oq + 1) * 1024],
                                      outh[gi % 2][:]
                                      ).then_inc(S["out%d" % (gi % 2)], 16)


        @block.vector
        def _(eng):
            ch_ = Chain(eng, S["chv"])
            eng.wait_ge(S["const"], NCONST_DMA)
            ch_.post(eng.memset(pad8[:, 4:8], -3.0e38))
            eng.wait_ge(S["iot"], 1)

            def ln_v_pre(p, cc):
                gch = p * CH + cc
                if gch >= 1:
                    eng.wait_ge(S["act"], 2 * gch)
                for half in range(2):
                    h = cc * 2 + half
                    eng.wait_ge(S["x%d" % (h % 2)],
                                16 * (p * CH + h // 2 + 1))
                    for sgi in range(4):
                        ch_.pre()
                        ch_.post(eng.bn_stats(
                            stat6[:, (half * 4 + sgi) * 6:
                                  (half * 4 + sgi + 1) * 6],
                            xf[h % 2][:, sgi * 512:(sgi + 1) * 512]))
                ch_.pre()
                ch_.post(eng.bn_aggr(mv2[:], stat6[:]))
                ch_.pre()
                ch_.post(eng.tensor_scalar(sv[:], mv2[:, 1:2], 1e-5,
                                           None, op0=ALU.add),
                         S["vpre"], gch + 1)
                ch_.pre()
                ch_.post(eng.tensor_scalar_mul(nmb[:], mv2[:, 0:1], -1.0))

            def ln_v_post(p, cc):
                gch = p * CH + cc
                eng.wait_ge(S["sqrt"], gch + 1)
                ch_.pre()
                ch_.post(eng.reciprocal(rstd[:], sv[:]), S["vstat"], gch + 1)
                for half in range(2):
                    h = cc * 2 + half
                    gi = p * H2 + h
                    if gi >= 2:
                        eng.wait_ge(S["xn2_%d" % (h % 2)],
                                    16 * 2 * (p * CH + h // 2))
                    ch_.pre()
                    ch_.post(eng.tensor_scalar(xnr[h % 2][:],
                                               xf[h % 2][:],
                                               nmb[:, 0:1], rstd[:, 0:1],
                                               op0=ALU.add, op1=ALU.mult),
                             S["act"], gi + 1)

            for cc in range(CH):
                ln_v_pre(0, cc)
                ln_v_post(0, cc)
            for p in range(NP):
                if p + 1 < NP:
                    for cc in range(CH):
                        ln_v_pre(p + 1, cc)
                        ln_v_post(p + 1, cc)

                if p >= 1:
                    eng.wait_ge(S["bmm"], p * 33 - 1)
                ch_.pre()
                ch_.post(eng.memset(mt[0][64:128, :], 0.0),
                         S["mz"], p * 2 + 1)

                if p >= 1:
                    eng.wait_ge(S["fmm"], p * 4 * CH)
                for k1 in range(33):
                    gi = p * 33 + k1
                    eng.wait_ge(S["xev"], gi + 1)
                    ch_.pre()
                    ch_.post(eng.tensor_mul(t1b[k1 % 2][:],
                                            xh[k1 % 2][0:64, :],
                                            xh[k1 % 2][0:64, :]))
                    eng.wait_ge(S["sq"], gi + 1)
                    ch_.pre()
                    ch_.post(eng.tensor_add(p_slot(k1), t1b[k1 % 2][:],
                                            t2b[k1 % 2][:]),
                             S["pev"], gi + 1)
                    if k1 == 30:
                        eng.wait_ge(S["bmm"], p * 33 + 31)
                        ch_.pre()
                        ch_.post(eng.memset(mt[0][64:128, :], 0.0),
                                 S["mz"], p * 2 + 2)

                eng.wait_ge(S["zmm"], p + 1)
                eng.wait_ge(S["lgc"], p + 1)
                ch_.pre()
                ch_.post(eng.tensor_copy(zt[:], pb[0][0:32, 0:PR]),
                         S["zev"], p + 1)

                def v_apply(p2, cc2):
                    par2 = cc2 % 2
                    gch2 = p2 * CH + cc2
                    base_applv = applv_cum(p2, cc2) - 8
                    eng.wait_ge(S["xal%d" % (cc2 % 2)],
                                16 * (CH // 2) * (p2 + 1))
                    dstx = xa[:, cc2 * 4096:(cc2 + 1) * 4096]
                    for j in range(8):
                        eng.wait_ge(S["gath%d" % (j % 2)],
                                    16 * (gch2 * 8 + 2 * (j // 2 + 1)))
                        ch_.pre()
                        ch_.post(eng.tensor_scalar(
                            roll[j % 2][:], roll[j % 2][:],
                            w8[par2][:, j:j + 1], None, op0=ALU.mult))
                        ch_.pre()
                        ch_.post(eng.tensor_tensor(dstx, dstx,
                                                   roll[j % 2][:], ALU.add),
                                 S["applv"], base_applv + j + 1)

                if p >= 1:
                    eng.wait_ge(S["t3"], p * CH)
                eng.wait_ge(S["lgt"], (p + 1) * CH)
                eng.wait_ge(S["t2"], (p + 1) * CH)
                for blk in range(CH):
                    ch_.pre()
                    ch_.post(eng.tensor_copy(
                        pad8[:, 0:4], lgt[:, blk * 4:(blk + 1) * 4]))
                    ch_.pre()
                    ch_.post(eng.max(gv8[:], pad8[:]))
                    ch_.pre()
                    ch_.post(eng.max_index(pidx[:], gv8[:], pad8[:]))
                    ch_.pre()
                    ch_.post(eng.tensor_copy(idxf[:], pidx[:, 0:1]))
                    ch_.pre()
                    ch_.post(eng.tensor_scalar(oh[:], iota4[:],
                                               idxf[:, 0:1],
                                               None,
                                               op0=ALU.is_equal))
                    ch_.pre()
                    ch_.post(eng.tensor_copy(
                        pzt[:, blk * 32:(blk + 1) * 32],
                        pb[1][:, 16 + blk * 32:16 + (blk + 1) * 32]),
                        S["pzt"], p * CH + blk + 1)
                    ch_.pre()
                    ch_.post(eng.tensor_tensor(
                        zgt[:, blk * 32:(blk + 1) * 32].rearrange(
                            "q (e r) -> q e r", e=4, r=8),
                        pzt[:, blk * 32:(blk + 1) * 32].rearrange(
                            "q (e r) -> q e r", e=4, r=8),
                        oh[:].to_broadcast([128, 4, 8]),
                        ALU.mult), S["zgt"], p * CH + blk + 1)

                for cc in range(CH):
                    par = cc % 2
                    gch = p * CH + cc
                    eng.wait_ge(S["cev"], gch * 8 + 8)
                    if gch >= 2:
                        eng.wait_ge(S["gath0"], 16 * 8 * (gch - 1))
                        eng.wait_ge(S["gath1"], 16 * 8 * (gch - 1))
                        appl_done_wait(eng, gch - 2)
                        eng.wait_ge(S["exp"], gch - 1)
                    ch_.pre()
                    ch_.post(eng.max(v8[par][:], corr[:]))
                    ch_.pre()
                    ch_.post(eng.max_index(idx8[par][:], v8[par][:], corr[:]))
                    ch_.pre()
                    ch_.post(eng.tensor_scalar(tt1[:], idx8[par][:], 63, None,
                                               op0=ALU.bitwise_and))
                    ch_.pre()
                    ch_.post(eng.tensor_scalar(tt1[:], tt1[:], 6, None,
                                               op0=ALU.logical_shift_left))
                    ch_.pre()
                    ch_.post(eng.tensor_scalar(tt2[:], idx8[par][:], 6, None,
                                               op0=ALU.logical_shift_right))
                    ch_.pre()
                    ch_.post(eng.tensor_tensor(tau[par][:], tt1[:], tt2[:],
                                               ALU.add))
                    ch_.pre()
                    ch_.post(eng.tensor_tensor(
                        offs[par][:], tau[par][:],
                        rowb[:, cc:cc + 1].to_broadcast(
                            [128, 8]), ALU.add))
                    ch_.pre()
                    ch_.post(eng.tensor_scalar_mul(negv0[par][:],
                                                   v8[par][:, 0:1], -1.0),
                             S["top"], 2 * gch + 2, inc=2)
                    eng.wait_ge(S["exp"], gch + 1)
                    ch_.pre()
                    ch_.post(eng.reduce_sum(s8[:], e8[par][:], axis=AX.X))
                    ch_.pre()
                    ch_.post(eng.reciprocal(r8[:], s8[:]))
                    ch_.pre()
                    ch_.post(eng.tensor_scalar(w8[par][:], e8[par][:],
                                               r8[:, 0:1], None,
                                               op0=ALU.mult),
                             S["sm"], gch + 1)
                    if cc >= 1:
                        v_apply(p, cc - 1)
                v_apply(p, CH - 1)

                for oq in range(4):
                    for rb in range(CH):
                        gi = (p * 4 + oq) * CH + rb
                        eng.wait_ge(S["fmm"], gi + 1)
                        appl_done_wait(eng, p * CH + rb)
                        if gi >= 2:
                            eng.wait_ge(S["out%d" % (gi % 2)],
                                        16 * (gi // 2))
                        for bk in range(2):
                            ch_.pre()
                            ins = eng.tensor_tensor(
                                outh[gi % 2][:, bk * 512:(bk + 1) * 512],
                                pb[2 * rb + bk][:],
                                xa[:, rb * 4096 + oq * 1024 + bk * 512:
                                   rb * 4096 + oq * 1024 + (bk + 1) * 512],
                                ALU.add)
                            if bk == 1:
                                ch_.post(ins, S["fev"], gi + 1)
                            else:
                                ch_.post(ins)


        @block.gpsimd
        def _(eng):
            ch_ = Chain(eng, S["chg"])
            eng.dma_start(agb[:].rearrange("p (c a) -> p c a", c=32, a=36),
                          ag_d.rearrange("(c p) a -> p c a", p=128)
                          ).then_inc(S["cg"], 16)
            eng.dma_start(bcatb[:], bcat_d).then_inc(S["cg"], 16)
            for ci in range(CH):
                ch_.pre()
                ins = eng.iota(rowb[:, ci:ci + 1], [[0, 1]],
                               base=ci * 128 * 8192, channel_multiplier=8192)
                if ci == CH - 1:
                    ch_.post(ins, S["iot"], 1)
                else:
                    ch_.post(ins)

            for wi in range(NW):
                oq, fc = divmod(wi, 32)
                eng.dma_start(
                    wscr_d[fc * 128:(fc + 1) * 128,
                           oq * 1024:(oq + 1) * 1024],
                    wt_d[fc * 128:(fc + 1) * 128,
                         oq * 1024:(oq + 1) * 1024]
                ).then_inc(S["wscr%d" % (wi % 2)], 16)

            for p in range(NP):
                xn2_flat = xn2_d[p].rearrange("(o a) b -> o (a b)", o=1)
                for cc in range(CH):
                    if p >= 1:
                        eng.wait_ge(S["fev"],
                                    (p - 1) * 4 * CH + 3 * CH + cc + 1)
                    eng.dma_start(
                        xa[:, cc * 4096:(cc + 1) * 4096],
                        x_d[p * PR + cc * 128:p * PR + (cc + 1) * 128, :]
                    ).then_inc(S["xal%d" % (cc % 2)], 16)

                for cc in range(CH):
                    par = cc % 2
                    gch = p * CH + cc
                    eng.wait_ge(S["sm"], gch + 1)
                    eng.wait_ge(S["xn2_0"], 16 * 2 * CH * (p + 1))
                    eng.wait_ge(S["xn2_1"], 16 * 2 * CH * (p + 1))
                    for j in range(8):
                        roll_free_wait(eng, gch * 8 + j)
                        for hf in range(2):
                            eng.indirect_dma_start(
                                roll[j % 2][:, hf * 2048:(hf + 1) * 2048],
                                None, xn2_flat,
                                IndirectOffsetOnAxis(offs[par][:, j:j + 1],
                                                     1),
                                element_offset=hf * 2048,
                            ).then_inc(S["gath%d" % (j % 2)], 16)


        @block.tensor
        def _(eng):
            eng.wait_ge(S["const"], NCONST_DMA)
            eng.wait_ge(S["cg"], 32)
            for p in range(NP):
                for cp in range(PAIRS):
                    for t in range(16):
                        gi = (p * PAIRS + cp) * 16 + t
                        if gi == p * PAIRS * 16 and p >= 1:
                            eng.wait_ge(S["fev"], p * 4 * CH - CH + 1)
                        if gi == p * PAIRS * 16 + 2 and p >= 1:
                            eng.wait_ge(S["fev"], p * 4 * CH - 1)
                        q = t // 4
                        eng.wait_ge(S["comb%d" % (q % 2)],
                                    16 * 2 * (p * PAIRS * 2 + cp * 2 +
                                              q // 2 + 1))
                        if gi >= 4:
                            eng.wait_ge(S["aev"], gi - 3)
                        mv = xat[q % 2][:, (t % 4) * 512:(t % 4 + 1) * 512]
                        eng.matmul(pb[(0, 1, 4, 5)[t % 4]][:], gab[:], mv,
                                   start=True, stop=True).then_inc(S["amm"], 1)

                for k1 in range(33):
                    gi = p * 33 + k1
                    if k1 == 0 and p >= 1:
                        eng.wait_ge(S["fev"], p * 4 * CH)
                    if k1 == 2 and p >= 1:
                        eng.wait_ge(S["cev"], 8 * CH * p)
                    eng.wait_ge(S["sbt%d" % (k1 % 2)],
                                16 * (p * SBT_PP[k1 % 2] + k1 // 2 + 1))
                    eng.wait_ge(S["bmov%d" % (k1 % 2)],
                                16 * (p * BMOV_PP[k1 % 2] + bmov_pcum[k1]))
                    if k1 in (0, 32):
                        eng.wait_ge(S["mz"], p * 2 + (1 if k1 == 0 else 2))
                    if gi >= 4:
                        eng.wait_ge(S["xev"], gi - 3)
                    eng.matmul(pb[(2, 3, 6, 7)[k1 % 4]][:, 0:PR],
                               sbtb[k1 % 2][:],
                               mt[k1 % 2][:], start=True, stop=True
                               ).then_inc(S["bmm"], 1)

                for k1 in range(33):
                    gi = p * 33 + k1
                    if k1 == 0 and p >= 1:
                        eng.wait_ge(S["cev"], 8 * CH * p)
                    eng.wait_ge(S["pev"], gi + 1)
                    if gi >= 2:
                        eng.wait_ge(S["eev"], gi - 1)
                    sib_h = sibb[64 * (k1 % 2):64 * (k1 % 2) + 64, :]
                    eng.matmul(pb[4 + k1 % 2][:, 0:PR], sib_h, p_slot(k1),
                               start=True, stop=True).then_inc(S["ibmm"], 1)

                eng.wait_ge(S["aev"], 16 * PAIRS * (p + 1))
                eng.wait_ge(S["xev"], 33 * (p + 1))
                eng.wait_ge(S["eev"], 33 * (p + 1))
                eng.wait_ge(S["xnfr"], 16 * 32 * (p + 1))
                for fc in range(32):
                    ins = eng.matmul(pb[0][0:36, 0:PR],
                                     agb[:, fc * 36:(fc + 1) * 36],
                                     arena[:, fc * PR:(fc + 1) * PR],
                                     start=(fc == 0), stop=(fc == 31))
                ins.then_inc(S["zmm"], 1)
                eng.wait_ge(S["lgc"], p + 1)
                for blk in range(CH):
                    eng.transpose(pb[1][:, blk * 4:(blk + 1) * 4],
                                  lg[:, blk * 128:(blk + 1) * 128],
                                  identf[0:4, 0:4]).then_inc(S["t1"], 1)
                eng.wait_ge(S["zev"], p + 1)
                eng.wait_ge(S["lgt"], (p + 1) * CH)
                for blk in range(CH):
                    eng.transpose(
                        pb[1][:, 16 + blk * 32:16 + (blk + 1) * 32],
                        zt[:, blk * 128:(blk + 1) * 128],
                        identf[0:32, 0:32]).then_inc(S["t2"], 1)

                for ch in range(CH):
                    gch = p * CH + ch
                    eng.wait_ge(S["et%d" % (gch % 2)],
                                16 * (gch // 2 + 1))
                    for g in range(8):
                        gi = (p * CH + ch) * 8 + g
                        if g == 0 and ch == 0:
                            pass
                        if gi >= 4:
                            eng.wait_ge(S["cev"], gi - 3)
                        for tl8 in range(8):
                            tl = g * 8 + tl8
                            ins = eng.matmul(
                                pb[4 + g % 4][:, tl8 * 64:(tl8 + 1) * 64],
                                et[gch % 2][:, tl * 128:(tl + 1) * 128],
                                miab[:, tl * 64:(tl + 1) * 64],
                                start=True, stop=True)
                        ins.then_inc(S["iamm"], 1)

                for blk in range(CH):
                    eng.wait_ge(S["zgt"], p * CH + blk + 1)
                    eng.transpose(pb[2][0:32, blk * 128:(blk + 1) * 128],
                                  zgt[:, blk * 32:(blk + 1) * 32],
                                  identf[:]).then_inc(S["t3"], 1)

                for oq in range(4):
                    for fc in range(32):
                        wi = p * NW + oq * 32 + fc
                        idx = oq * 32 + fc
                        eng.wait_ge(S["w%d" % (wi % 3)],
                                    16 * (wi // 3 + 1))
                        if fc == 0:
                            if oq == 0:
                                eng.wait_ge(S["cev"], (p + 1) * CH * 8)
                                eng.wait_ge(S["lgt"], (p + 1) * CH)
                                eng.wait_ge(S["pzt"], (p + 1) * CH)
                                eng.wait_ge(S["zgs"], (p + 1) * CH)
                        for rb in range(CH):
                            if fc == 0 and oq >= 1:
                                eng.wait_ge(S["fev"],
                                            p * 4 * CH + (oq - 1) * CH +
                                            rb + 1)
                            lhs = arena[:, fc * PR + rb * 128:
                                        fc * PR + (rb + 1) * 128]
                            for bk in range(2):
                                ins = eng.matmul(
                                    pb[2 * rb + bk][:], lhs,
                                    wtile[wi % 3][:, bk * 512:(bk + 1) * 512],
                                    start=(fc == 0), stop=False)
                        ins.then_inc(S["wcons"], 1)
                    for rb in range(CH):
                        lhs = zgs[:, rb * 128:(rb + 1) * 128]
                        for bk in range(2):
                            ins = eng.matmul(
                                pb[2 * rb + bk][:], lhs,
                                bcatb[:, oq * 1024 + bk * 512:
                                      oq * 1024 + (bk + 1) * 512],
                                start=False, stop=True)
                        ins.then_inc(S["fmm"], 1)

    es.close()
    return nc


def kernel(L0_attn_x, L0_ffn_W_up, L0_ffn_gate_w, L0_ffn_expert_bias,
           L0_ffn_lora_A, L0_ffn_lora_B):
    x = np.ascontiguousarray(np.asarray(L0_attn_x).reshape(-1, L))
    wt = np.ascontiguousarray(np.asarray(L0_ffn_W_up).T)
    ag = np.ascontiguousarray(np.concatenate(
        [np.asarray(L0_ffn_lora_A).reshape(32, L),
         np.asarray(L0_ffn_gate_w)], axis=0).T)
    bcat = np.ascontiguousarray(
        np.asarray(L0_ffn_lora_B).transpose(0, 2, 1).reshape(32, L))
    eb = np.ascontiguousarray(
        np.asarray(L0_ffn_expert_bias).reshape(4, 1).astype(np.float32,
                                                            copy=False))
    tabs = _host_tables()

    cfg = Cfg(NP=4, CH=4)
    nc = build_nc(cfg)
    shards = np.split(x, NCORES, axis=0)
    in_maps = []
    for i in range(NCORES):
        m = {"x": np.ascontiguousarray(shards[i]), "wt": wt, "ag": ag,
             "bcat": bcat, "ebias": eb}
        m.update(tabs)
        in_maps.append(m)
    res = run_bass_kernel_spmd(nc, in_maps, core_ids=list(range(NCORES)))
    out = np.concatenate([np.asarray(r["out"]) for r in res.results], axis=0)
    return out.reshape(4, L, L).astype(np.float32, copy=False)
